# version 25
# baseline (speedup 1.0000x reference)
import contextlib
import numpy as np

import concourse.bass as bass
import concourse.mybir as mybir
from concourse.bass_utils import run_bass_kernel_spmd

F32 = mybir.dt.float32
BF16 = mybir.dt.bfloat16
AF = mybir.ActivationFunctionType
ALU = mybir.AluOpType

P = 128
D = 1024
KD = 8
S = 4096
SC = 512
NCH = S // SC
NJ = SC // P

C1 = 0.7978845608028654
C3 = C1 * 0.044715
EPS = 1e-5

SEMS = ("cp", "cwT", "g1", "g2", "g3", "gsp", "xld", "x2", "p2g", "p2n",
        "p2h", "sxg", "sxnb", "shb", "yst", "pe", "act", "dve", "gp")
OWN = {"scalar": "act", "vector": "dve", "tensor": "pe", "gpsimd": "gp"}


class Plan:
    def __init__(self):
        self.q = {n: [] for n in ("sync", "scalar", "vector", "tensor", "gpsimd")}
        self.cnt = {s: 0 for s in SEMS}
        self.ev = {}
        self.skipped = []

    def emit(self, qn, fns, deps=(), inc=None, label=None):
        if not isinstance(fns, (list, tuple)):
            fns = [fns]
        ws = {}
        for d in deps:
            if d is None:
                continue
            if isinstance(d, tuple) and len(d) == 3 and d[0] == "#":
                s, c = d[1], d[2]
            else:
                if d not in self.ev:
                    self.skipped.append(d)
                    continue
                s, c = self.ev[d]
            if c > ws.get(s, 0):
                ws[s] = c
        if inc is not None:
            s, n = inc
            self.cnt[s] += n
            if label is not None:
                self.ev[label] = (s, self.cnt[s])
        self.q[qn].append((list(fns), ws, inc, label))

    def run(self, qn, eng, sems):
        hw = {}
        own = OWN.get(qn)
        for fns, ws, inc, _lb in self.q[qn]:
            for s in SEMS:
                c = ws.get(s)
                if c is None or s == own or hw.get(s, 0) >= c:
                    continue
                eng.wait_ge(sems[s], c)
                hw[s] = c
            inst = None
            for f in fns:
                inst = f(eng)
            if inc is not None and inst is not None:
                inst.then_inc(sems[inc[0]], inc[1])


def build_nc(detect_races=False, ph2=True, ph1=True, nocast=False,
             stages="LB23CDEFGHIWouspatfy", dbg=False):
    nc = bass.Bass("TRN2", target_bir_lowering=False,
                   detect_race_conditions=detect_races)

    x_h = nc.dram_tensor("x", [S, D], F32, kind="ExternalInput").ap()
    w_h = {}
    for nm in ("wf1", "wf2", "wbx", "wgx", "wga", "wb2", "wT"):
        w_h[nm] = nc.dram_tensor(nm, [D, D], F32, kind="ExternalInput").ap()
    dna_h = nc.dram_tensor("dna", [100, D], F32, kind="ExternalInput").ap()
    uaT_h = nc.dram_tensor("uaT", [50, D], F32, kind="ExternalInput").ap()
    ldT_h = nc.dram_tensor("ldT", [100, 4], F32, kind="ExternalInput").ap()
    luT_h = nc.dram_tensor("luT", [4, 50], F32, kind="ExternalInput").ap()
    packs_h = nc.dram_tensor("packs", [P, 72], F32, kind="ExternalInput").ap()
    afwbc_h = nc.dram_tensor("afwbc", [P, KD * SC], F32, kind="ExternalInput").ap()
    ident_h = nc.dram_tensor("ident", [P, P], F32, kind="ExternalInput").ap()
    y_h = nc.dram_tensor("y", [S, D], F32, kind="ExternalOutput").ap()
    KND = "ExternalOutput" if dbg else "Internal"
    xg_h = nc.dram_tensor("xgs", [P, NCH, KD * SC], BF16, kind=KND).ap()
    xnb_h = nc.dram_tensor("xnbs", [P, NCH, KD * SC], BF16, kind=KND).ap()
    hb_h = nc.dram_tensor("hbs", [P, NCH, KD * SC],
                          F32 if nocast else BF16, kind=KND).ap()
    wc_h = nc.dram_tensor("wcs", [P, KD, D], BF16, kind=KND).ap()
    if dbg:
        dbg_h = nc.dram_tensor("dbgs", [P, NCH, 16], F32,
                               kind="ExternalOutput").ap()

    x_r = x_h.rearrange("(c j p) d -> c p j d", p=P, j=NJ)
    y_r = y_h.rearrange("(c j p) d -> c p j d", p=P, j=NJ)
    w_r = {nm: w_h[nm].rearrange("(k p) n -> k p n", p=P) for nm in w_h}

    align = lambda o: (o + 31) & ~31
    off = align(nc.sbuf_base)

    def at(name, shape, dtype, o):
        return nc.alloc_sbuf_tensor_at(name, shape, dtype, offset=o)

    def nbytes(shape, dtype):
        return int(np.prod(shape[1:])) * (4 if dtype == F32 else 2)

    def persist(name, shape, dtype):
        nonlocal off
        t = at(name, shape, dtype, off)
        off = align(off + nbytes(shape, dtype))
        return t

    ident = persist("identt", [P, P], F32)
    q25 = persist("q25c", [P, 1], F32)
    dsum = persist("dsumc", [P, KD], F32)
    packs = persist("packs_s", [P, 72], F32)
    dhalf = persist("dhalf", [P, 72], F32)
    dwh = persist("dwh", [P, 72], F32)
    dwh2 = persist("dwh2", [P, 72], F32)
    carry = persist("carryb", [P, KD * 32], F32)
    mvm = [persist(f"mvm{i}", [P, 64], F32) for i in range(2)]
    ssq = [persist(f"ssq{i}", [P, 64], F32) for i in range(2)]
    smv = [persist(f"smv{i}", [P, 64], F32) for i in range(2)]
    tmp4 = [persist(f"tmp4{i}", [P, 64], F32) for i in range(2)]
    rstd = [persist(f"rstd{i}", [P, 64], F32) for i in range(2)]

    region = align(off)
    TOP = nc.sbuf_top

    o = region
    offs1 = {}

    def ph(name, shape, dtype):
        nonlocal o
        offs1[name] = (o, nbytes(shape, dtype))
        t = at(name, shape, dtype, o)
        o = align(o + nbytes(shape, dtype))
        return t

    xbuf = [ph(f"xb{i}", [P, NJ, D], F32) for i in range(2)]
    sc_xn = [ph(f"scxn{i}", [P, SC], F32) for i in range(5)]
    sc_a = [ph(f"sca{i}", [P, SC], F32) for i in range(2)]
    sq = [ph(f"sq{i}", [P, SC], F32) for i in range(2)]
    junk4 = ph("junk4", [P, NJ, D], F32)
    xg = [ph(f"xg{i}", [P, KD * SC], BF16) for i in range(2)]
    xnbb = ph("xnbb", [P, KD * SC], BF16)
    vb = [ph(f"vb{i}", [P, KD * SC], BF16) for i in range(2)]
    thx = [ph(f"thx{i}", [P, KD * SC], F32) for i in range(2)]
    abuf = ph("abuf", [P, KD * SC], F32)
    wP1 = {nm: ph(f"w_{nm}", [P, KD, D], BF16)
           for nm in ("wbx", "wgx", "wga")}
    assert o <= TOP, f"phase1 sbuf overflow {o} > {TOP}"

    so = offs1["abuf"][0]

    def su(name, shape, dtype):
        nonlocal so
        t = at(name, shape, dtype, so)
        so = align(so + nbytes(shape, dtype))
        return t

    t1b = su("t1b", [4, D], BF16)
    t2b = su("t2b", [50, D], BF16)
    ldTb = su("ldTb", [100, 4], BF16)
    luTb = su("luTb", [4, 50], BF16)
    dnab = su("dnab", [100, D], BF16)
    uaTb = su("uaTb", [50, D], BF16)
    assert so <= offs1["abuf"][0] + offs1["abuf"][1]

    def p2a(name, shape, dtype, src, boff=0):
        o0 = offs1[src][0] + boff
        return at(name, shape, dtype, o0)

    wP2 = {}
    wP2["wf1"] = p2a("w2_wf1", [P, KD, D], BF16, "xb0")
    wP2["wc"] = p2a("w2_wc", [P, KD, D], BF16, "xb1")
    wP2["wf2"] = p2a("w2_wf2", [P, KD, D], BF16, "xg0")
    wP2["wb2"] = p2a("w2_wb2", [P, KD, D], BF16, "w_wbx")
    xgin = [p2a(f"xgin{i}", [P, KD * SC], BF16, "w_wgx", i * 8192)
            for i in range(2)]
    xnbin = [p2a(f"xnbin{i}", [P, KD * SC], BF16, "w_wga", i * 8192)
             for i in range(2)]
    hbin = [p2a(f"hbin{i}", [P, KD * SC], BF16, "thx0", i * 8192)
            for i in range(2)]
    xin = [p2a("xin0", [P, NJ, D], F32, "thx1"),
           p2a("xin1", [P, NJ, D], F32, "abuf")]
    ubuf = p2a("ubuf", [P, KD * SC], F32, "vb0")
    hfb = xgin
    prs = [p2a(f"prs{i}", [P, SC], F32, "xnbb", i * 2048) for i in range(4)]
    afw = p2a("afw", [P, KD * SC], F32, "scxn0")
    gbuf = p2a("gbufb", [P, KD * SC], BF16, "junk4", 0)
    ssx = [p2a(f"ssx{i}", [P, SC], F32, "junk4", 8192 + i * 2048)
           for i in range(2)]

    pT0 = nc.alloc_psum_tensor("pT0", [P, SC], F32)
    pA = [nc.alloc_psum_tensor(f"pA{i}", [P, SC], F32) for i in range(2)]
    pG = [nc.alloc_psum_tensor(f"pG{i}", [P, SC], F32) for i in range(2)]
    pV = [nc.alloc_psum_tensor(f"pV{i}", [P, SC], F32) for i in range(2)]
    pP2 = nc.alloc_psum_tensor("pP2", [P, SC], F32)
    pS, pU, pF = pA, pG, pV

    pl = Plan()
    E = pl.emit

    def pk(v, k):
        return packs[:, v * KD + k:v * KD + k + 1]
    BF1, BF2, BBX, BGX, BGA, BB2, BC, AFW, APR = range(9)

    def hbgx_c(k):
        return dhalf[:, BGX * KD + k:BGX * KD + k + 1]

    def hbga_c(k):
        return dhalf[:, BGA * KD + k:BGA * KD + k + 1]

    def whalf_c(k):
        return dwh[:, APR * KD + k:APR * KD + k + 1]

    def whalf2_c(k):
        return dwh2[:, APR * KD + k:APR * KD + k + 1]

    vq = lambda q: 0 <= q < NCH

    def cdup(q):
        return NCH - 1 - q

    def ksl(buf, k):
        return buf[:, k * SC:(k + 1) * SC]

    E("sync", lambda e: e.dma_start(out=packs[:], in_=packs_h),
      inc=("cp", 16))
    E("sync", lambda e: e.dma_start(out=ident[:], in_=ident_h),
      inc=("cp", 16), label="Lcp")
    for nm in ("wbx", "wgx", "wga"):
        for k in range(KD):
            E("gpsimd",
              (lambda nm=nm, k=k: lambda e: e.dma_start(
                  out=wP1[nm][:, k, :], in_=w_r[nm][k]))(),
              inc=("g1", 16), label=("wld", nm, k))
    for nm, dst, src in (("ldT", ldTb, ldT_h), ("luT", luTb, luT_h),
                         ("dna", dnab, dna_h), ("uaT", uaTb, uaT_h)):
        E("gpsimd", (lambda dst=dst, src=src: lambda e: e.dma_start(
            out=dst[:], in_=src))(), inc=("g1", 16), label=("wld", nm))
    for k in range(KD):
        dst = thx[k // 4]
        kk = k % 4
        E("sync", (lambda dst=dst, kk=kk, k=k: lambda e: e.dma_start(
            out=dst[:, kk * D:(kk + 1) * D], in_=w_r["wT"][k]))(),
          inc=("cwT", 16), label=("wTld", k))

    G1TOT = ("#", "g1", pl.cnt["g1"])
    CWTOT = ("#", "cwT", pl.cnt["cwT"])
    E("scalar", lambda e: e.activation(dwh[:], packs[:], AF.Exp),
      deps=["Lcp"], inc=("act", 1), label="su1")
    E("vector", lambda e: e.tensor_scalar(dwh[:], dwh[:], 1.0, None, ALU.add),
      deps=["su1"], inc=("dve", 1), label="su2")
    E("scalar", lambda e: e.activation(dwh[:], dwh[:], AF.Ln),
      deps=["su2"], inc=("act", 1), label="su3")
    E("vector", lambda e: e.tensor_scalar(dwh[:], dwh[:], -4.0, None, ALU.mult),
      deps=["su3"], inc=("dve", 1), label="su4")
    E("vector", lambda e: e.tensor_scalar(dwh2[:], dwh[:], 2.0, None, ALU.mult),
      deps=["su4"], inc=("dve", 1), label="su4b")
    E("vector", lambda e: e.tensor_scalar(dhalf[:], packs[:], 0.5, None, ALU.mult),
      deps=["Lcp"], inc=("dve", 1), label="su5")
    E("vector", lambda e: e.memset(q25[:], 0.25),
      deps=[], inc=("dve", 1), label="q25")
    E("vector", lambda e: e.tensor_tensor(
        dsum[:], packs[:, BF2 * KD:BF2 * KD + KD],
        packs[:, BB2 * KD:BB2 * KD + KD], ALU.add),
      deps=["Lcp"], inc=("dve", 1), label="dsu")
    for bi, buf in enumerate((smv[0], smv[1], ssq[0], ssq[1])):
        E("vector", (lambda buf=buf: lambda e: e.tensor_scalar(
            buf[:], ident[:, 0:64], 0.0, None, ALU.mult))(),
          deps=["Lcp"], inc=("dve", 1), label=("szero", bi))

    psu = [pT0, pA[0]]
    for h in range(2):
        E("tensor", (lambda h=h: lambda e: e.matmul(
            psu[h][0:4, :], ldTb[:], dnab[:, h * SC:(h + 1) * SC],
            start=True, stop=True))(),
          deps=[G1TOT], inc=("pe", 1), label=("suT1", h))
        E("vector", (lambda h=h: lambda e: e.tensor_copy(
            t1b[:, h * SC:(h + 1) * SC], psu[h][0:4, :]))(),
          deps=[("suT1", h)], inc=("dve", 1), label=("suT1e", h))
    for h in range(2):
        E("tensor", (lambda h=h: lambda e: e.matmul(
            psu[h][0:50, :], luTb[:], t1b[:, h * SC:(h + 1) * SC],
            start=True, stop=True))(),
          deps=[G1TOT, ("suT1e", 0), ("suT1e", 1)], inc=("pe", 1),
          label=("suT2", h))
        E("vector", (lambda h=h: lambda e: e.tensor_copy(
            t2b[:, h * SC:(h + 1) * SC], psu[h][0:50, :]))(),
          deps=[("suT2", h)], inc=("dve", 1), label=("suT2e", h))
    for half in range(2):
        for it4 in range(4):
            it = half * 4 + it4
            wtsrc = thx[it // 4]
            wslice = it % 4
            for h in range(2):
                E("tensor", (lambda it=it, h=h: lambda e: e.matmul(
                    psu[h][:], t2b[:, it * P:(it + 1) * P],
                    uaTb[:, h * SC:(h + 1) * SC], start=True, stop=True))(),
                  deps=[G1TOT, CWTOT, ("suT2e", 0), ("suT2e", 1),
                        ("suWc", half, it4 - 1, h) if it4 > 0
                        else ("suWc", half - 1, 3, h)],
                  inc=("pe", 1), label=("suL", it, h))
                E("vector", (lambda it=it, h=h, wtsrc=wtsrc, ws=wslice,
                             half=half, it4=it4: lambda e:
                             e.scalar_tensor_tensor(
                                 vb[half][:].rearrange("p (a n) -> p a n", a=8)
                                 [:, it4 * 2 + h, :],
                                 psu[h][:], 1.0,
                                 wtsrc[:, ws * D + h * SC: ws * D + (h + 1) * SC],
                                 ALU.mult, ALU.add))(),
                  deps=[("suL", it, h)], inc=("dve", 1),
                  label=("suWc", half, it4, h))
        E("gpsimd", (lambda half=half: lambda e: e.dma_start(
            out=wc_h[:, half * 4:(half + 1) * 4, :].rearrange("p a n -> p (a n)"),
            in_=vb[half][:]))(),
          deps=[("suWc", half, 3, 1)],
          inc=("gsp", 16), label=("suSp", half))
    setup_dve = ("#", "dve", pl.cnt["dve"])
    setup_gsp = ("#", "gsp", pl.cnt["gsp"])
    setup_pe = ("#", "pe", pl.cnt["pe"])

    for i in range(-2, (NCH + 3) if ph1 else -2):
        qx = i + 2
        qt = i + 1
        qv = i
        qg = i - 1
        qa = i - 2
        qh = i - 3

        if "D" in stages and vq(qh):
            E("gpsimd", (lambda qh=qh: lambda e: e.dma_start(
                out=hb_h[:, cdup(qh), :], in_=thx[qh % 2][:]))(),
              deps=[("scan", qh, 7), ("sphb", qh - 1)],
              inc=("shb", 16), label=("sphb", qh))

        if "L" in stages and vq(qx):
            E("sync", (lambda qx=qx: lambda e: e.dma_start(
                out=xbuf[qx % 2][:], in_=x_r[cdup(qx)]))(),
              deps=[("T", qx - 2, 7), ("xld", qx - 1)],
              inc=("xld", 16), label=("xld", qx))

        if "B" in stages and vq(qt):
            def gelu_k(k, qt):
                E("scalar", (lambda k=k: lambda e: e.activation(
                    sc_a[k % 2][:], sc_xn[k % 5][:], AF.Square))(),
                  deps=[("Tev2", qt, k),
                        ("th", qt, k - 2) if k >= 2 else
                        ("th", qt - 1, k + 6),
                        ("xg", qt, k - 2) if k >= 2 else
                        ("xg", qt - 1, k + 6)],
                  inc=("act", 1), label=("gsq", qt, k))
                E("vector", (lambda k=k: lambda e: e.tensor_scalar(
                    sc_a[k % 2][:], sc_a[k % 2][:], 8.0 * C3, 2.0 * C1,
                    ALU.mult, ALU.add))(),
                  deps=[("gsq", qt, k)], inc=("dve", 1), label=("gq", qt, k))
                E("vector", (lambda k=k: lambda e: e.tensor_tensor(
                    sc_a[k % 2][:], sc_a[k % 2][:], sc_xn[k % 5][:],
                    ALU.mult))(),
                  deps=[], inc=("dve", 1), label=("gt", qt, k))
                E("scalar", (lambda k=k: lambda e: e.activation(
                    sc_a[k % 2][:], sc_a[k % 2][:], AF.Tanh))(),
                  deps=[("gt", qt, k),
                        ("xg", qt, k - 2) if k >= 2 else
                        ("xg", qt - 1, k + 6)],
                  inc=("act", 1), label=("th", qt, k))
                E("vector", (lambda qt=qt, k=k: lambda e:
                             e.scalar_tensor_tensor(
                                 ksl(xg[qt % 2], k), sc_a[k % 2][:], 1.0,
                                 sc_xn[k % 5][:], ALU.add, ALU.mult))(),
                  deps=[("th", qt, k), ("V", qt - 2, 7), ("spxg", qt - 2)],
                  inc=("dve", 1), label=("xg", qt, k))
            for k in range(KD):
                fns = [(lambda qt=qt, k=k, j=j: lambda e: e.transpose(
                    pT0[:, j * P:(j + 1) * P],
                    xbuf[qt % 2][:, j, k * P:(k + 1) * P], ident[:]))()
                    for j in list(range(NJ)) + [NJ - 1]]
                E("tensor", fns,
                  deps=[("nrm", qt, NJ - 1), ("Tev3", qt, k - 1),
                        "Lcp", setup_dve],
                  inc=("pe", 1), label=("T", qt, k))
                E("scalar", (lambda qt=qt, k=k: lambda e: e.activation(
                    sc_xn[k % 5][:], pT0[:], AF.Identity, scale=0.5))(),
                  deps=[("T", qt, k),
                        ("gt", qt, k - 5) if k >= 5 else
                        ("gt", qt - 1, k + 5 if k < 3 else k),
                        ("xg", qt, k - 5) if k >= 5 else
                        ("xg", qt - 1, k + 5 if k < 3 else k)],
                  inc=("act", 1), label=("Tev2", qt, k))
                E("scalar", (lambda qt=qt, k=k: lambda e: e.activation(
                    ksl(xnbb, k), pT0[:], AF.Identity))(),
                  deps=[("spxnb", qt - 1)],
                  inc=("act", 1), label=("Tev3", qt, k))
                if k == 3 and "G" in stages and vq(qx):
                    for j in range(NJ):
                        E("gpsimd", (lambda qx=qx, j=j: lambda e:
                                     e.tensor_tensor(
                                         junk4[:, j, :], xbuf[qx % 2][:, j, :],
                                         xbuf[qx % 2][:, j, :], ALU.mult))(),
                          deps=[("xld", qx), ("ssq2", qx - 1, NJ - 1)],
                          inc=("gp", 1), label=("sqa2", qx, j))
                if "2" in stages and k >= 3:
                    gelu_k(k - 3, qt)
            if "2" in stages:
                for k in range(KD - 3, KD):
                    gelu_k(k, qt)
            if "3" in stages:
                E("scalar", (lambda qt=qt: lambda e: e.dma_start(
                    out=xnb_h[:, cdup(qt), :], in_=xnbb[:]))(),
                  deps=[("Tev3", qt, 7), ("spxnb", qt - 1)],
                  inc=("sxnb", 16), label=("spxnb", qt))
                E("scalar", (lambda qt=qt: lambda e: e.dma_start(
                    out=xg_h[:, cdup(qt), :], in_=xg[qt % 2][:]))(),
                  deps=[("xg", qt, 7), ("spxg", qt - 1)],
                  inc=("sxg", 16), label=("spxg", qt))
        elif "B" in stages and vq(qx):
            for j in range(NJ):
                E("gpsimd", (lambda qx=qx, j=j: lambda e: e.tensor_tensor(
                    junk4[:, j, :], xbuf[qx % 2][:, j, :],
                    xbuf[qx % 2][:, j, :], ALU.mult))(),
                  deps=[("xld", qx), ("ssq2", qx - 1, NJ - 1)],
                  inc=("gp", 1), label=("sqa2", qx, j))

        if "C" in stages and vq(qa):
            for m in range(KD):
                fns = [(lambda qa=qa, m=m, kk=kk: lambda e: e.matmul(
                    pA[m % 2][:], wP1["wga"][:, kk, m * P:(m + 1) * P],
                    ksl(vb[qa % 2], kk), start=(kk == 0), stop=(kk == KD - 1)))()
                    for kk in range(KD)]
                E("tensor", fns,
                  deps=[("thx", qa, 7), ("ga1", qa, m - 2), G1TOT, setup_dve],
                  inc=("pe", 1), label=("GA", qa, m))
                E("scalar", (lambda qa=qa, m=m: lambda e: e.activation(
                    ksl(abuf, m), pA[m % 2][:], AF.Tanh, bias=hbga_c(m),
                    scale=0.5))(),
                  deps=[("GA", qa, m), ("scan", qa - 1, 7),
                        ("sqa", qa - 1, m), setup_pe],
                  inc=("act", 1), label=("ga1", qa, m))
                E("scalar", (lambda qa=qa, m=m: lambda e: e.activation(
                    ksl(abuf, m), ksl(abuf, m), AF.Exp,
                    bias=whalf_c(m), scale=whalf_c(m)))(),
                  deps=["su4", setup_dve], inc=("act", 1), label=("aw", qa, m))
            E("vector", (lambda qa=qa: lambda e: e.scalar_tensor_tensor(
                thx[qa % 2][:], thx[qa % 2][:], 1.0, vb[qa % 2][:],
                ALU.add, ALU.mult))(),
              deps=[("GA", qa, 7), ("sphb", qa - 2)],
              inc=("dve", 1), label=("gxv", qa))

        if "E" in stages and vq(qg):
            for m in range(KD):
                fns = [(lambda qg=qg, m=m, kk=kk: lambda e: e.matmul(
                    pG[m % 2][:], wP1["wgx"][:, kk, m * P:(m + 1) * P],
                    ksl(vb[qg % 2], kk), start=(kk == 0), stop=(kk == KD - 1)))()
                    for kk in range(KD)]
                E("tensor", fns,
                  deps=[("vev", qg, 7), ("thx", qg, m - 2), G1TOT],
                  inc=("pe", 1), label=("GX", qg, m))
                E("scalar", (lambda qg=qg, m=m: lambda e: e.activation(
                    ksl(thx[qg % 2], m), pG[m % 2][:], AF.Tanh,
                    bias=hbgx_c(m), scale=0.5))(),
                  deps=[("GX", qg, m), ("sphb", qg - 2), setup_pe],
                  inc=("act", 1), label=("thx", qg, m))

        if "G" in stages and vq(qx):
            for j in range(NJ):
                E("vector", (lambda qx=qx, j=j: lambda e: e.tensor_reduce(
                    smv[qx % 2][:, j:j + 1], xbuf[qx % 2][:, j, :],
                    mybir.AxisListType.X, ALU.add))(),
                  deps=[("xld", qx)], inc=("dve", 1), label=("smv2", qx, j))
                E("vector", (lambda qx=qx, j=j: lambda e: e.tensor_reduce(
                    ssq[qx % 2][:, j:j + 1], junk4[:, j, :],
                    mybir.AxisListType.X, ALU.add))(),
                  deps=[("sqa2", qx, j)], inc=("dve", 1), label=("ssq2", qx, j))
            E("vector", (lambda qx=qx: lambda e: e.tensor_scalar(
                mvm[qx % 2][:], smv[qx % 2][:], 1.0 / 1024.0, None, ALU.mult))(),
              deps=[], inc=("dve", 1), label=("agM", qx))
            E("vector", (lambda qx=qx: lambda e: e.tensor_tensor(
                tmp4[qx % 2][:], mvm[qx % 2][:], mvm[qx % 2][:], ALU.mult))(),
              deps=[("nrm", qx - 2, NJ - 1)], inc=("dve", 1), label=("agE", qx))
            E("vector", (lambda qx=qx: lambda e: e.scalar_tensor_tensor(
                rstd[qx % 2][:], ssq[qx % 2][:], 1.0 / 1024.0, tmp4[qx % 2][:],
                ALU.mult, ALU.subtract))(),
              deps=[("ssq2", qx, NJ - 1), ("nrm", qx - 2, NJ - 1)],
              inc=("dve", 1), label=("agV", qx))
            E("vector", (lambda qx=qx: lambda e: e.tensor_scalar(
                rstd[qx % 2][:], rstd[qx % 2][:], EPS, None, ALU.add))(),
              deps=[], inc=("dve", 1), label=("veps", qx))
            E("scalar", (lambda qx=qx: lambda e: e.activation(
                tmp4[qx % 2][:], rstd[qx % 2][:], AF.Sqrt))(),
              deps=[("veps", qx)], inc=("act", 1), label=("rsq", qx))
            E("vector", (lambda qx=qx: lambda e: e.reciprocal(
                rstd[qx % 2][:], tmp4[qx % 2][:]))(),
              deps=[("rsq", qx)], inc=("dve", 1), label=("rcp", qx))
            E("vector", (lambda qx=qx: lambda e: e.scalar_tensor_tensor(
                tmp4[qx % 2][:], mvm[qx % 2][:], -1.0, rstd[qx % 2][:],
                ALU.mult, ALU.mult))(),
              deps=[("rcp", qx)], inc=("dve", 1), label=("ngm", qx))
            if dbg:
                E("scalar", (lambda qx=qx: lambda e: e.dma_start(
                    out=dbg_h[:, cdup(qx), 0:4], in_=mvm[qx % 2][:, 0:4]))(),
                  deps=[("agM", qx)], inc=("cp", 16))
                E("scalar", (lambda qx=qx: lambda e: e.dma_start(
                    out=dbg_h[:, cdup(qx), 4:8], in_=rstd[qx % 2][:, 0:4]))(),
                  deps=[("rcp", qx)], inc=("cp", 16))
                E("scalar", (lambda qx=qx: lambda e: e.dma_start(
                    out=dbg_h[:, cdup(qx), 8:12], in_=ssq[qx % 2][:, 0:4]))(),
                  deps=[("agV", qx)], inc=("cp", 16))

        if "I" in stages and vq(qx):
            for j in range(NJ):
                E("scalar", (lambda qx=qx, j=j: lambda e: e.activation(
                    xbuf[qx % 2][:, j, :], xbuf[qx % 2][:, j, :], AF.Identity,
                    bias=tmp4[qx % 2][:, j:j + 1],
                    scale=rstd[qx % 2][:, j:j + 1]))(),
                  deps=[("ngm", qx)], inc=("act", 1), label=("nrm", qx, j))

        if "F" in stages and vq(qv):
            for m in range(KD):
                fns = [(lambda qv=qv, m=m, kk=kk: lambda e: e.matmul(
                    pV[m % 2][:], wP1["wbx"][:, kk, m * P:(m + 1) * P],
                    ksl(xg[qv % 2], kk), start=(kk == 0), stop=(kk == KD - 1)))()
                    for kk in range(KD)]
                E("tensor", fns,
                  deps=[("xg", qv, 7), ("vev", qv, m - 2), G1TOT],
                  inc=("pe", 1), label=("V", qv, m))
                E("vector", (lambda qv=qv, m=m: lambda e: e.tensor_scalar(
                    ksl(vb[qv % 2], m), pV[m % 2][:], pk(BBX, m), None,
                    ALU.add))(),
                  deps=[("V", qv, m), ("GA", qv - 2, 7), ("gxv", qv - 2),
                        setup_gsp],
                  inc=("dve", 1), label=("vev", qv, m))
        if "H" in stages and vq(qa):
            for k in range(KD):
                E("scalar", (lambda qa=qa, k=k: lambda e: e.activation(
                    sq[k % 2][:], ksl(abuf, k), AF.Square))(),
                  deps=[("aw", qa, k), ("si", qa, k - 2), setup_pe],
                  inc=("act", 1), label=("sqa", qa, k))
                E("scalar", (lambda qa=qa, k=k: lambda e: e.activation(
                    sq[k % 2][:], sq[k % 2][:], AF.Sqrt,
                    bias=q25[:, 0:1], scale=-0.25))(),
                  deps=[("sqa", qa, k), "q25", ("si", qa, k - 2)],
                  inc=("act", 1), label=("sc2", qa, k))
                E("vector", (lambda qa=qa, k=k: lambda e: e.tensor_tensor(
                    ksl(thx[qa % 2], k), ksl(thx[qa % 2], k), sq[k % 2][:],
                    ALU.mult))(),
                  deps=[("sc2", qa, k), ("gxv", qa)],
                  inc=("dve", 1), label=("si", qa, k))
            for k in range(KD):
                ini = 0.0 if qa == 0 else carry[:, k * 32:k * 32 + 1]
                E("vector", (lambda qa=qa, k=k, ini=ini: lambda e:
                             e.tensor_tensor_scan(
                                 ksl(thx[qa % 2], k)[:, ::-1],
                                 ksl(abuf, k)[:, ::-1],
                                 ksl(thx[qa % 2], k)[:, ::-1],
                                 ini, ALU.mult, ALU.add))(),
                  deps=[("si", qa, k), ("cv", qa - 1)],
                  inc=("dve", 1), label=("scan", qa, k))
            E("vector", (lambda qa=qa: lambda e: e.tensor_copy(
                carry[:].rearrange("p (k n) -> p k n", k=KD),
                thx[qa % 2][:].rearrange("p (k n) -> p k n", k=KD)
                [:, :, 0:32]))(),
              deps=[("scan", qa, 7)], inc=("dve", 1), label=("cv", qa))

        if "W" in stages and i == NCH - 1:
            for nm, g in (("wf1", [("T", NCH - 1, 7)]),
                          ("wf2", [("V", NCH - 1, 7), ("spxg", NCH - 1)]),
                          ("wb2", [("V", NCH - 1, 7)])):
                for k in range(KD):
                    E("gpsimd", (lambda nm=nm, k=k: lambda e: e.dma_start(
                        out=wP2[nm][:, k, :], in_=w_r[nm][k]))(),
                      deps=g, inc=("g2", 16), label=("w2", nm, k))

    P1_LAST = [("sphb", NCH - 1)]

    for k in range(KD):
        E("sync", (lambda k=k: lambda e: e.dma_start(
            out=wP2["wc"][:, k, :], in_=wc_h[:, k, :]))(),
          deps=[("T", NCH - 1, 7), ("#", "gsp", 32)],
          inc=("g3", 16), label=("w2", "wc", k))
    G2TOT = ("#", "g2", pl.cnt["g2"])
    G3TOT = ("#", "g3", pl.cnt["g3"])
    E("sync", lambda e: e.dma_start(out=afw[:], in_=afwbc_h),
      deps=[("si", NCH - 1, 7), ("sc2", NCH - 1, 7), ("th", NCH - 1, 7),
            ("xg", NCH - 1, 7), ("ssq2", NCH - 1, NJ - 1)],
      inc=("x2", 16), label="afb")
    JG = ("ssq2", NCH - 1, NJ - 1)

    for i in range(-1, (NCH + 1) if ph2 else -1):
        jl = i + 1
        jc = i
        jf = i - 1

        def ofob_half(h, jf):
            for m in range(h * 4, h * 4 + 4):
                fns = [(lambda jf=jf, m=m, kk=kk: lambda e: e.matmul(
                    pS[m % 2][:], wP2["wf2"][:, kk, m * P:(m + 1) * P],
                    ksl(hfb[jf % 2], kk), start=(kk == 0), stop=False))()
                    for kk in range(KD)]
                fns += [(lambda jf=jf, m=m, kk=kk: lambda e: e.matmul(
                    pS[m % 2][:], wP2["wb2"][:, kk, m * P:(m + 1) * P],
                    ksl(hbin[jf % 2], kk), start=False, stop=(kk == KD - 1)))()
                    for kk in range(KD)]
                E("tensor", fns,
                  deps=[("hf", jf), ("p2hb", jf), ("ssum", jf, m - 2), G2TOT],
                  inc=("pe", 1), label=("OF", jf, m))
                E("vector", (lambda jf=jf, m=m: lambda e:
                             e.scalar_tensor_tensor(
                                 ssx[m % 2][:], pS[m % 2][:],
                                 dsum[:, m:m + 1], ksl(xnbin[jf % 2], m),
                                 ALU.add, ALU.add))(),
                  deps=[("OF", jf, m), ("prod", jf, m - 2),
                        ("prod", jf - 1, m), "dsu"]
                       + ([JG] if jf == 0 else []),
                  inc=("dve", 1), label=("ssum", jf, m))
                E("gpsimd", (lambda jf=jf, m=m: lambda e: e.tensor_tensor(
                    prs[m % 4][:], ssx[m % 2][:], ksl(gbuf, m), ALU.mult))(),
                  deps=[("ssum", jf, m), ("gev", jf, m)]
                       + ([("Tp", jf, 3)] if m >= 4
                          else [("Tp", jf - 1, 7)]),
                  inc=("gp", 1), label=("prod", jf, m))
        if "o" in stages and vq(jf):
            ofob_half(0, jf)

        if "u" in stages and vq(jc):
            for m in range(KD):
                fns = [(lambda jc=jc, m=m, kk=kk: lambda e: e.matmul(
                    pU[m % 2][:], wP2["wf1"][:, kk, m * P:(m + 1) * P],
                    ksl(xgin[jc % 2], kk), start=(kk == 0), stop=(kk == KD - 1)))()
                    for kk in range(KD)]
                E("tensor", fns,
                  deps=[("p2xg", jc), ("uev", jc, m - 2), G2TOT]
                       + ([("thx", NCH - 1, 7)] if jc == 0 else []),
                  inc=("pe", 1), label=("U", jc, m))
                E("scalar", (lambda jc=jc, m=m: lambda e: e.activation(
                    ksl(ubuf, m), pU[m % 2][:], AF.Identity,
                    bias=pk(BF1, m)))(),
                  deps=[("U", jc, m), ("p2cv", jc - 1), ("hf", jc - 1)]
                       + ([("gxv", NCH - 1), ("GA", NCH - 1, 7)]
                          if jc == 0 else []),
                  inc=("act", 1), label=("uev", jc, m))
            for k in range(KD):
                ini = 0.0 if jc == 0 else carry[:, k * 32 + 31:k * 32 + 32]
                E("vector", (lambda k=k, ini=ini, jc=jc: lambda e:
                             e.tensor_tensor_scan(
                                 ksl(ubuf, k), ksl(afw, k), ksl(ubuf, k),
                                 ini, ALU.mult, ALU.add))(),
                  deps=[("uev", jc, 7), "afb", ("p2cv", jc - 1)],
                  inc=("dve", 1), label=("p2scan", jc, k))
            E("vector", (lambda jc=jc: lambda e: e.tensor_copy(
                carry[:].rearrange("p (k n) -> p k n", k=KD),
                ubuf[:].rearrange("p (k n) -> p k n", k=KD)
                [:, :, SC - 32:SC]))(),
              deps=[("p2scan", jc, 7)], inc=("dve", 1), label=("p2cv", jc))
            E("scalar", (lambda jc=jc: lambda e: e.activation(
                hfb[jc % 2][:], ubuf[:], AF.Identity))(),
              deps=[("p2scan", jc, 7), ("OF", jc - 2, 7)],
              inc=("act", 1), label=("hf", jc))

        def tp_half(h, jf):
            for jj in range(NJ):
                g = h * 4 + jj
                bank = [pT0, pP2][g % 2]
                fns = [(lambda jj=jj, kk=kk, bank=bank: lambda e: e.transpose(
                    bank[:, kk * P:(kk + 1) * P],
                    prs[kk][:, jj * P:(jj + 1) * P], ident[:]))()
                    for kk in list(range(4)) + [3]]
                E("tensor", fns,
                  deps=[("prod", jf, h * 4 + 3), ("oadd", jf, g - 2), "Lcp"],
                  inc=("pe", 1), label=("Tp", jf, g))
                E("vector", (lambda jf=jf, jj=jj, h=h, bank=bank: lambda e:
                             e.tensor_tensor(
                                 xin[jf % 2][:, jj, h * SC:(h + 1) * SC],
                                 xin[jf % 2][:, jj, h * SC:(h + 1) * SC],
                                 bank[:], ALU.add))(),
                  deps=[("Tp", jf, h * 4 + jj), ("p2x", jf)],
                  inc=("dve", 1), label=("oadd", jf, h * 4 + jj))
        if "t" in stages and vq(jf):
            tp_half(0, jf)
        if "o" in stages and vq(jf):
            ofob_half(1, jf)

        if "f" in stages and vq(jc):
            for m in range(KD):
                fns = [(lambda jc=jc, m=m, kk=kk: lambda e: e.matmul(
                    pF[m % 2][:], wP2["wc"][:, kk, m * P:(m + 1) * P],
                    ksl(xnbin[jc % 2], kk), start=(kk == 0), stop=(kk == KD - 1)))()
                    for kk in range(KD)]
                E("tensor", fns,
                  deps=[("p2xnb", jc), ("gev", jc, m - 2), G3TOT],
                  inc=("pe", 1), label=("FF", jc, m))
                E("scalar", (lambda jc=jc, m=m: lambda e: e.activation(
                    ksl(gbuf, m), pF[m % 2][:], AF.Silu, bias=pk(BC, m)))(),
                  deps=[("FF", jc, m), ("prod", jc - 1, m)]
                       + ([JG] if jc == 0 else []),
                  inc=("act", 1), label=("gev", jc, m))

        if "t" in stages and vq(jf):
            tp_half(1, jf)

        if "y" in stages and vq(jf):
            E("scalar", (lambda jf=jf: lambda e: e.dma_start(
                out=y_r[jf], in_=xin[jf % 2][:]))(),
              deps=[("oadd", jf, 7), ("yst", jf - 1)],
              inc=("yst", 16), label=("yst", jf))

        if "s" in stages and vq(jl):
            E("sync", (lambda jl=jl: lambda e: e.dma_start(
                out=xgin[jl % 2][:], in_=xg_h[:, jl, :]))(),
              deps=[("U", jl - 2, 7), ("OF", jl - 2, 7), ("p2xg", jl - 1),
                    ("GX", NCH - 1, 7), ("spxg", NCH - 1 - jl)],
              inc=("p2g", 16), label=("p2xg", jl))
            E("sync", (lambda jl=jl: lambda e: e.dma_start(
                out=xnbin[jl % 2][:], in_=xnb_h[:, jl, :]))(),
              deps=[("FF", jl - 2, 7), ("ssum", jl - 2, 7), ("p2xnb", jl - 1),
                    ("GA", NCH - 1, 7), ("spxnb", NCH - 1 - jl)],
              inc=("p2n", 16), label=("p2xnb", jl))
            E("sync", (lambda jl=jl: lambda e: e.dma_start(
                out=hbin[jl % 2][:], in_=hb_h[:, jl, :]))(),
              deps=[("OF", jl - 2, 7), ("p2hb", jl - 1),
                    ("sphb", NCH - 1 - jl)],
              inc=("p2h", 16), label=("p2hb", jl))
            E("sync", (lambda jl=jl: lambda e: e.dma_start(
                out=xin[jl % 2][:], in_=x_r[jl]))(),
              deps=[("yst", jl - 2), ("p2x", jl - 1),
                    ("scan", NCH - 1, 7), ("sqa", NCH - 1, 7)] + P1_LAST,
              inc=("x2", 16), label=("p2x", jl))

    fwd = [d for d in pl.skipped if d in pl.ev]
    assert not fwd, f"forward-referenced deps (ordering bugs): {sorted(set(fwd))[:20]}"
    build_nc._last_plan = pl

    E("sync", lambda e: e.nop(),
      deps=[("#", "yst", pl.cnt["yst"]), ("#", "shb", pl.cnt["shb"]),
            ("#", "sxg", pl.cnt["sxg"]), ("#", "sxnb", pl.cnt["sxnb"]),
            ("#", "gsp", pl.cnt["gsp"]), ("#", "g2", pl.cnt["g2"]),
            ("#", "g3", pl.cnt["g3"])], inc=None)

    es = contextlib.ExitStack()
    with es:
        sems = {s: es.enter_context(nc.semaphore(f"sem_{s}")) for s in SEMS}
        block = es.enter_context(nc.Block())

        @block.sync
        def _(eng):
            pl.run("sync", eng, sems)

        @block.scalar
        def _(eng):
            pl.run("scalar", eng, sems)

        @block.vector
        def _(eng):
            pl.run("vector", eng, sems)

        @block.tensor
        def _(eng):
            pl.run("tensor", eng, sems)

        @block.gpsimd
        def _(eng):
            pl.run("gpsimd", eng, sems)

    return nc


def kernel(**inputs):
    x = np.asarray(inputs["L0_attn_x"], dtype=np.float32)
    vecs = [np.asarray(inputs[k], dtype=np.float32) for k in (
        "L0_attn_bf1", "L0_attn_bf2", "L0_attn_bbx", "L0_attn_bgx",
        "L0_attn_bga", "L0_attn_bb2", "L0_ffn_b", "L0_attn_a_fwd",
        "L0_attn_a_param")]
    packs = np.concatenate([v.reshape(KD, P).T for v in vecs], axis=1)
    packs = np.ascontiguousarray(packs, dtype=np.float32)
    base = {
        "wf1": np.ascontiguousarray(inputs["L0_attn_Wf1"], np.float32),
        "wf2": np.ascontiguousarray(inputs["L0_attn_Wf2"], np.float32),
        "wbx": np.ascontiguousarray(inputs["L0_attn_Wbx"], np.float32),
        "wgx": np.ascontiguousarray(inputs["L0_attn_Wgx"], np.float32),
        "wga": np.ascontiguousarray(inputs["L0_attn_Wga"], np.float32),
        "wb2": np.ascontiguousarray(inputs["L0_attn_Wb2"], np.float32),
        "wT": np.ascontiguousarray(np.asarray(inputs["L0_ffn_W"]).T, np.float32),
        "dna": np.ascontiguousarray(inputs["L0_ffn_down_aux"], np.float32),
        "uaT": np.ascontiguousarray(np.asarray(inputs["L0_ffn_up_aux"]).T,
                                    np.float32),
        "ldT": np.ascontiguousarray(np.asarray(inputs["L0_ffn_lora_down"]).T,
                                    np.float32),
        "luT": np.ascontiguousarray(np.asarray(inputs["L0_ffn_lora_up"]).T,
                                    np.float32),
        "packs": packs,
        "ident": np.eye(P, dtype=np.float32),
        "afwbc": np.ascontiguousarray(np.repeat(
            np.asarray(inputs["L0_attn_a_fwd"], np.float32).reshape(KD, P).T
            [:, :, None], SC, axis=2).reshape(P, KD * SC)),
    }
    nc = build_nc()
    in_maps = []
    for b in range(8):
        m = dict(base)
        m["x"] = np.ascontiguousarray(x[b], np.float32)
        in_maps.append(m)
    res = run_bass_kernel_spmd(nc, in_maps, core_ids=list(range(8)))
    return np.stack([r["y"] for r in res.results], axis=0)


# version 26
# speedup vs baseline: 1.2051x; 1.2051x over previous
import contextlib
import numpy as np

import concourse.bass as bass
import concourse.mybir as mybir
from concourse.bass_utils import run_bass_kernel_spmd

F32 = mybir.dt.float32
BF16 = mybir.dt.bfloat16
AF = mybir.ActivationFunctionType
ALU = mybir.AluOpType

P = 128
D = 1024
KD = 8
S = 4096
SC = 512
NCH = S // SC
NJ = SC // P

C1 = 0.7978845608028654
C3 = C1 * 0.044715
EPS = 1e-5

SEMS = ("cp", "cwT", "g1", "g2", "g3", "gsp", "xld", "x2", "p2g", "p2n",
        "p2h", "sxg", "sxnb", "shb", "yst", "pe", "act", "dve", "gp")
OWN = {"scalar": "act", "vector": "dve", "tensor": "pe", "gpsimd": "gp"}


class Plan:
    def __init__(self):
        self.q = {n: [] for n in ("sync", "scalar", "vector", "tensor", "gpsimd")}
        self.cnt = {s: 0 for s in SEMS}
        self.ev = {}
        self.skipped = []

    def emit(self, qn, fns, deps=(), inc=None, label=None):
        if not isinstance(fns, (list, tuple)):
            fns = [fns]
        ws = {}
        for d in deps:
            if d is None:
                continue
            if isinstance(d, tuple) and len(d) == 3 and d[0] == "#":
                s, c = d[1], d[2]
            else:
                if d not in self.ev:
                    self.skipped.append(d)
                    continue
                s, c = self.ev[d]
            if c > ws.get(s, 0):
                ws[s] = c
        if inc is not None:
            s, n = inc
            self.cnt[s] += n
            if label is not None:
                self.ev[label] = (s, self.cnt[s])
        self.q[qn].append((list(fns), ws, inc, label))

    def run(self, qn, eng, sems):
        hw = {}
        own = OWN.get(qn)
        for fns, ws, inc, _lb in self.q[qn]:
            for s in SEMS:
                c = ws.get(s)
                if c is None or s == own or hw.get(s, 0) >= c:
                    continue
                eng.wait_ge(sems[s], c)
                hw[s] = c
            inst = None
            for f in fns:
                inst = f(eng)
            if inc is not None and inst is not None:
                inst.then_inc(sems[inc[0]], inc[1])


def build_nc(detect_races=False, ph2=True, ph1=True, nocast=False,
             stages="LB23CDEFGHIWouspatfy", dbg=False):
    nc = bass.Bass("TRN2", target_bir_lowering=False,
                   detect_race_conditions=detect_races)

    x_h = nc.dram_tensor("x", [S, D], F32, kind="ExternalInput").ap()
    w_h = {}
    for nm in ("wf1", "wf2", "wbx", "wgx", "wga", "wb2", "wT"):
        w_h[nm] = nc.dram_tensor(nm, [D, D], F32, kind="ExternalInput").ap()
    dna_h = nc.dram_tensor("dna", [100, D], F32, kind="ExternalInput").ap()
    uaT_h = nc.dram_tensor("uaT", [50, D], F32, kind="ExternalInput").ap()
    ldT_h = nc.dram_tensor("ldT", [100, 4], F32, kind="ExternalInput").ap()
    luT_h = nc.dram_tensor("luT", [4, 50], F32, kind="ExternalInput").ap()
    packs_h = nc.dram_tensor("packs", [P, 72], F32, kind="ExternalInput").ap()
    afwbc_h = nc.dram_tensor("afwbc", [P, KD * SC], F32, kind="ExternalInput").ap()
    ident_h = nc.dram_tensor("ident", [P, P], F32, kind="ExternalInput").ap()
    y_h = nc.dram_tensor("y", [S, D], F32, kind="ExternalOutput").ap()
    KND = "ExternalOutput" if dbg else "Internal"
    xg_h = nc.dram_tensor("xgs", [P, NCH, KD * SC], BF16, kind=KND).ap()
    xnb_h = nc.dram_tensor("xnbs", [P, NCH, KD * SC], BF16, kind=KND).ap()
    hb_h = nc.dram_tensor("hbs", [P, NCH, KD * SC],
                          F32 if nocast else BF16, kind=KND).ap()
    wc_h = nc.dram_tensor("wcs", [P, KD, D], BF16, kind=KND).ap()
    if dbg:
        dbg_h = nc.dram_tensor("dbgs", [P, NCH, 16], F32,
                               kind="ExternalOutput").ap()

    x_r = x_h.rearrange("(c j p) d -> c p j d", p=P, j=NJ)
    y_r = y_h.rearrange("(c j p) d -> c p j d", p=P, j=NJ)
    w_r = {nm: w_h[nm].rearrange("(k p) n -> k p n", p=P) for nm in w_h}

    align = lambda o: (o + 31) & ~31
    off = align(nc.sbuf_base)

    def at(name, shape, dtype, o):
        return nc.alloc_sbuf_tensor_at(name, shape, dtype, offset=o)

    def nbytes(shape, dtype):
        return int(np.prod(shape[1:])) * (4 if dtype == F32 else 2)

    def persist(name, shape, dtype):
        nonlocal off
        t = at(name, shape, dtype, off)
        off = align(off + nbytes(shape, dtype))
        return t

    ident = persist("identt", [P, P], F32)
    q25 = persist("q25c", [P, 1], F32)
    dsum = persist("dsumc", [P, KD], F32)
    packs = persist("packs_s", [P, 72], F32)
    dhalf = persist("dhalf", [P, 72], F32)
    dwh = persist("dwh", [P, 72], F32)
    dwh2 = persist("dwh2", [P, 72], F32)
    carry = persist("carryb", [P, KD * 32], F32)
    mvm = [persist(f"mvm{i}", [P, 64], F32) for i in range(2)]
    ssq = [persist(f"ssq{i}", [P, 64], F32) for i in range(2)]
    smv = [persist(f"smv{i}", [P, 64], F32) for i in range(2)]
    tmp4 = [persist(f"tmp4{i}", [P, 64], F32) for i in range(2)]
    rstd = [persist(f"rstd{i}", [P, 64], F32) for i in range(2)]

    region = align(off)
    TOP = nc.sbuf_top

    o = region
    offs1 = {}

    def ph(name, shape, dtype):
        nonlocal o
        offs1[name] = (o, nbytes(shape, dtype))
        t = at(name, shape, dtype, o)
        o = align(o + nbytes(shape, dtype))
        return t

    xbuf = [ph(f"xb{i}", [P, NJ, D], F32) for i in range(2)]
    sc_xn = [ph(f"scxn{i}", [P, SC], F32) for i in range(5)]
    sc_a = [ph(f"sca{i}", [P, SC], F32) for i in range(2)]
    sq = [ph(f"sq{i}", [P, SC], F32) for i in range(2)]
    junk4 = ph("junk4", [P, NJ, D], F32)
    xg = [ph(f"xg{i}", [P, KD * SC], BF16) for i in range(2)]
    xnbb = ph("xnbb", [P, KD * SC], BF16)
    vb = [ph(f"vb{i}", [P, KD * SC], BF16) for i in range(2)]
    thx = [ph(f"thx{i}", [P, KD * SC], F32) for i in range(2)]
    abuf = ph("abuf", [P, KD * SC], F32)
    wP1 = {nm: ph(f"w_{nm}", [P, KD, D], BF16)
           for nm in ("wbx", "wgx", "wga")}
    assert o <= TOP, f"phase1 sbuf overflow {o} > {TOP}"

    so = offs1["abuf"][0]

    def su(name, shape, dtype):
        nonlocal so
        t = at(name, shape, dtype, so)
        so = align(so + nbytes(shape, dtype))
        return t

    t1b = su("t1b", [4, D], BF16)
    t2b = su("t2b", [50, D], BF16)
    ldTb = su("ldTb", [100, 4], BF16)
    luTb = su("luTb", [4, 50], BF16)
    dnab = su("dnab", [100, D], BF16)
    uaTb = su("uaTb", [50, D], BF16)
    assert so <= offs1["abuf"][0] + offs1["abuf"][1]

    def p2a(name, shape, dtype, src, boff=0):
        o0 = offs1[src][0] + boff
        return at(name, shape, dtype, o0)

    wP2 = {}
    wP2["wf1"] = p2a("w2_wf1", [P, KD, D], BF16, "xb0")
    wP2["wc"] = p2a("w2_wc", [P, KD, D], BF16, "xb1")
    wP2["wf2"] = p2a("w2_wf2", [P, KD, D], BF16, "xg0")
    wP2["wb2"] = p2a("w2_wb2", [P, KD, D], BF16, "w_wbx")
    xgin = [p2a(f"xgin{i}", [P, KD * SC], BF16, "w_wgx", i * 8192)
            for i in range(2)]
    xnbin = [p2a(f"xnbin{i}", [P, KD * SC], BF16, "w_wga", i * 8192)
             for i in range(2)]
    hbin = [p2a(f"hbin{i}", [P, KD * SC], BF16, "thx0", i * 8192)
            for i in range(2)]
    xin = [p2a("xin0", [P, NJ, D], F32, "thx1"),
           p2a("xin1", [P, NJ, D], F32, "abuf")]
    ubuf = p2a("ubuf", [P, KD * SC], F32, "vb0")
    hfb = xgin
    prs = [p2a(f"prs{i}", [P, SC], F32, "xnbb", i * 2048) for i in range(4)]
    afw = p2a("afw", [P, KD * SC], F32, "scxn0")
    gbuf = p2a("gbufb", [P, KD * SC], BF16, "junk4", 0)
    ssx = [p2a(f"ssx{i}", [P, SC], F32, "junk4", 8192 + i * 2048)
           for i in range(2)]

    pT0 = nc.alloc_psum_tensor("pT0", [P, SC], F32)
    pA = [nc.alloc_psum_tensor(f"pA{i}", [P, SC], F32) for i in range(2)]
    pG = [nc.alloc_psum_tensor(f"pG{i}", [P, SC], F32) for i in range(2)]
    pV = [nc.alloc_psum_tensor(f"pV{i}", [P, SC], F32) for i in range(2)]
    pP2 = nc.alloc_psum_tensor("pP2", [P, SC], F32)
    pS, pU, pF = pA, pG, pV

    pl = Plan()
    E = pl.emit

    def pk(v, k):
        return packs[:, v * KD + k:v * KD + k + 1]
    BF1, BF2, BBX, BGX, BGA, BB2, BC, AFW, APR = range(9)

    def hbgx_c(k):
        return dhalf[:, BGX * KD + k:BGX * KD + k + 1]

    def hbga_c(k):
        return dhalf[:, BGA * KD + k:BGA * KD + k + 1]

    def whalf_c(k):
        return dwh[:, APR * KD + k:APR * KD + k + 1]

    def whalf2_c(k):
        return dwh2[:, APR * KD + k:APR * KD + k + 1]

    vq = lambda q: 0 <= q < NCH

    def cdup(q):
        return NCH - 1 - q

    def ksl(buf, k):
        return buf[:, k * SC:(k + 1) * SC]

    E("sync", lambda e: e.dma_start(out=packs[:], in_=packs_h),
      inc=("cp", 16))
    E("sync", lambda e: e.dma_start(out=ident[:], in_=ident_h),
      inc=("cp", 16), label="Lcp")
    for nm in ("wbx", "wgx", "wga"):
        for k in range(KD):
            E("gpsimd",
              (lambda nm=nm, k=k: lambda e: e.dma_start(
                  out=wP1[nm][:, k, :], in_=w_r[nm][k]))(),
              inc=("g1", 16), label=("wld", nm, k))
    for nm, dst, src in (("ldT", ldTb, ldT_h), ("luT", luTb, luT_h),
                         ("dna", dnab, dna_h), ("uaT", uaTb, uaT_h)):
        E("gpsimd", (lambda dst=dst, src=src: lambda e: e.dma_start(
            out=dst[:], in_=src))(), inc=("g1", 16), label=("wld", nm))
    for k in range(KD):
        dst = thx[k // 4]
        kk = k % 4
        E("sync", (lambda dst=dst, kk=kk, k=k: lambda e: e.dma_start(
            out=dst[:, kk * D:(kk + 1) * D], in_=w_r["wT"][k]))(),
          inc=("cwT", 16), label=("wTld", k))

    G1TOT = ("#", "g1", pl.cnt["g1"])
    CWTOT = ("#", "cwT", pl.cnt["cwT"])
    E("scalar", lambda e: e.activation(dwh[:], packs[:], AF.Exp),
      deps=["Lcp"], inc=("act", 1), label="su1")
    E("vector", lambda e: e.tensor_scalar(dwh[:], dwh[:], 1.0, None, ALU.add),
      deps=["su1"], inc=("dve", 1), label="su2")
    E("scalar", lambda e: e.activation(dwh[:], dwh[:], AF.Ln),
      deps=["su2"], inc=("act", 1), label="su3")
    E("vector", lambda e: e.tensor_scalar(dwh[:], dwh[:], -4.0, None, ALU.mult),
      deps=["su3"], inc=("dve", 1), label="su4")
    E("vector", lambda e: e.tensor_scalar(dwh2[:], dwh[:], 2.0, None, ALU.mult),
      deps=["su4"], inc=("dve", 1), label="su4b")
    E("vector", lambda e: e.tensor_scalar(dhalf[:], packs[:], 0.5, None, ALU.mult),
      deps=["Lcp"], inc=("dve", 1), label="su5")
    E("vector", lambda e: e.memset(q25[:], 0.25),
      deps=[], inc=("dve", 1), label="q25")
    E("vector", lambda e: e.tensor_tensor(
        dsum[:], packs[:, BF2 * KD:BF2 * KD + KD],
        packs[:, BB2 * KD:BB2 * KD + KD], ALU.add),
      deps=["Lcp"], inc=("dve", 1), label="dsu")
    for bi, buf in enumerate((smv[0], smv[1], ssq[0], ssq[1])):
        E("vector", (lambda buf=buf: lambda e: e.tensor_scalar(
            buf[:], ident[:, 0:64], 0.0, None, ALU.mult))(),
          deps=["Lcp"], inc=("dve", 1), label=("szero", bi))

    psu = [pT0, pA[0]]
    for h in range(2):
        E("tensor", (lambda h=h: lambda e: e.matmul(
            psu[h][0:4, :], ldTb[:], dnab[:, h * SC:(h + 1) * SC],
            start=True, stop=True))(),
          deps=[G1TOT], inc=("pe", 1), label=("suT1", h))
        E("vector", (lambda h=h: lambda e: e.tensor_copy(
            t1b[:, h * SC:(h + 1) * SC], psu[h][0:4, :]))(),
          deps=[("suT1", h)], inc=("dve", 1), label=("suT1e", h))
    for h in range(2):
        E("tensor", (lambda h=h: lambda e: e.matmul(
            psu[h][0:50, :], luTb[:], t1b[:, h * SC:(h + 1) * SC],
            start=True, stop=True))(),
          deps=[G1TOT, ("suT1e", 0), ("suT1e", 1)], inc=("pe", 1),
          label=("suT2", h))
        E("vector", (lambda h=h: lambda e: e.tensor_copy(
            t2b[:, h * SC:(h + 1) * SC], psu[h][0:50, :]))(),
          deps=[("suT2", h)], inc=("dve", 1), label=("suT2e", h))
    for half in range(2):
        for it4 in range(4):
            it = half * 4 + it4
            wtsrc = thx[it // 4]
            wslice = it % 4
            for h in range(2):
                E("tensor", (lambda it=it, h=h: lambda e: e.matmul(
                    psu[h][:], t2b[:, it * P:(it + 1) * P],
                    uaTb[:, h * SC:(h + 1) * SC], start=True, stop=True))(),
                  deps=[G1TOT, CWTOT, ("suT2e", 0), ("suT2e", 1),
                        ("suWc", half, it4 - 1, h) if it4 > 0
                        else ("suWc", half - 1, 3, h)],
                  inc=("pe", 1), label=("suL", it, h))
                E("vector", (lambda it=it, h=h, wtsrc=wtsrc, ws=wslice,
                             half=half, it4=it4: lambda e:
                             e.scalar_tensor_tensor(
                                 vb[half][:].rearrange("p (a n) -> p a n", a=8)
                                 [:, it4 * 2 + h, :],
                                 psu[h][:], 1.0,
                                 wtsrc[:, ws * D + h * SC: ws * D + (h + 1) * SC],
                                 ALU.mult, ALU.add))(),
                  deps=[("suL", it, h)], inc=("dve", 1),
                  label=("suWc", half, it4, h))
        E("gpsimd", (lambda half=half: lambda e: e.dma_start(
            out=wc_h[:, half * 4:(half + 1) * 4, :].rearrange("p a n -> p (a n)"),
            in_=vb[half][:]))(),
          deps=[("suWc", half, 3, 1)],
          inc=("gsp", 16), label=("suSp", half))
    setup_dve = ("#", "dve", pl.cnt["dve"])
    setup_gsp = ("#", "gsp", pl.cnt["gsp"])
    setup_pe = ("#", "pe", pl.cnt["pe"])

    for i in range(-2, (NCH + 3) if ph1 else -2):
        qx = i + 2
        qt = i + 1
        qv = i
        qg = i - 1
        qa = i - 2
        qh = i - 3

        if "D" in stages and vq(qh):
            E("gpsimd", (lambda qh=qh: lambda e: e.dma_start(
                out=hb_h[:, cdup(qh), :], in_=thx[qh % 2][:]))(),
              deps=[("scan", qh, 7), ("sphb", qh - 1)],
              inc=("shb", 16), label=("sphb", qh))

        if "L" in stages and vq(qx):
            E("sync", (lambda qx=qx: lambda e: e.dma_start(
                out=xbuf[qx % 2][:], in_=x_r[cdup(qx)]))(),
              deps=[("T", qx - 2, 7), ("xld", qx - 1)],
              inc=("xld", 16), label=("xld", qx))

        if "B" in stages and vq(qt):
            def gelu_k(k, qt):
                E("vector", (lambda k=k: lambda e: e.tensor_tensor(
                    sc_a[k % 2][:], sc_xn[k % 5][:], sc_xn[k % 5][:],
                    ALU.mult))(),
                  deps=[("Tev2", qt, k),
                        ("th", qt, k - 2) if k >= 2 else
                        ("th", qt - 1, k + 6)],
                  inc=("dve", 1), label=("gsq", qt, k))
                E("vector", (lambda k=k: lambda e: e.tensor_scalar(
                    sc_a[k % 2][:], sc_a[k % 2][:], 8.0 * C3, 2.0 * C1,
                    ALU.mult, ALU.add))(),
                  deps=[], inc=("dve", 1), label=("gq", qt, k))
                E("vector", (lambda k=k: lambda e: e.tensor_tensor(
                    sc_a[k % 2][:], sc_a[k % 2][:], sc_xn[k % 5][:],
                    ALU.mult))(),
                  deps=[], inc=("dve", 1), label=("gt", qt, k))
                E("scalar", (lambda k=k: lambda e: e.activation(
                    sc_a[k % 2][:], sc_a[k % 2][:], AF.Tanh))(),
                  deps=[("gt", qt, k),
                        ("xg", qt, k - 2) if k >= 2 else
                        ("xg", qt - 1, k + 6)],
                  inc=("act", 1), label=("th", qt, k))
                E("vector", (lambda qt=qt, k=k: lambda e:
                             e.scalar_tensor_tensor(
                                 ksl(xg[qt % 2], k), sc_a[k % 2][:], 1.0,
                                 sc_xn[k % 5][:], ALU.add, ALU.mult))(),
                  deps=[("th", qt, k), ("V", qt - 2, 7), ("spxg", qt - 2)],
                  inc=("dve", 1), label=("xg", qt, k))
            for k in range(KD):
                fns = [(lambda qt=qt, k=k, j=j: lambda e: e.transpose(
                    pT0[:, j * P:(j + 1) * P],
                    xbuf[qt % 2][:, j, k * P:(k + 1) * P], ident[:]))()
                    for j in range(NJ)]
                E("tensor", fns,
                  deps=[("nrm", qt, NJ - 1), ("Tev3", qt, k - 1),
                        "Lcp", setup_dve],
                  inc=("pe", 1), label=("T", qt, k))
                E("scalar", (lambda qt=qt, k=k: lambda e: e.activation(
                    sc_xn[k % 5][:], pT0[:], AF.Identity, scale=0.5))(),
                  deps=[("T", qt, k),
                        ("gt", qt, k - 5) if k >= 5 else
                        ("gt", qt - 1, k + 5 if k < 3 else k),
                        ("xg", qt, k - 5) if k >= 5 else
                        ("xg", qt - 1, k + 5 if k < 3 else k)],
                  inc=("act", 1), label=("Tev2", qt, k))
                E("scalar", (lambda qt=qt, k=k: lambda e: e.activation(
                    ksl(xnbb, k), pT0[:], AF.Identity))(),
                  deps=[("spxnb", qt - 1)],
                  inc=("act", 1), label=("Tev3", qt, k))
                if k == 3 and "G" in stages and vq(qx):
                    for j in range(NJ):
                        E("gpsimd", (lambda qx=qx, j=j: lambda e:
                                     e.tensor_tensor(
                                         junk4[:, j, :], xbuf[qx % 2][:, j, :],
                                         xbuf[qx % 2][:, j, :], ALU.mult))(),
                          deps=[("xld", qx), ("ssq2", qx - 1, NJ - 1)],
                          inc=("gp", 1), label=("sqa2", qx, j))
                if "2" in stages and k >= 3:
                    gelu_k(k - 3, qt)
            if "2" in stages:
                for k in range(KD - 3, KD):
                    gelu_k(k, qt)
            if "3" in stages:
                E("scalar", (lambda qt=qt: lambda e: e.dma_start(
                    out=xnb_h[:, cdup(qt), :], in_=xnbb[:]))(),
                  deps=[("Tev3", qt, 7), ("spxnb", qt - 1)],
                  inc=("sxnb", 16), label=("spxnb", qt))
                E("scalar", (lambda qt=qt: lambda e: e.dma_start(
                    out=xg_h[:, cdup(qt), :], in_=xg[qt % 2][:]))(),
                  deps=[("xg", qt, 7), ("spxg", qt - 1)],
                  inc=("sxg", 16), label=("spxg", qt))
        elif "B" in stages and vq(qx):
            for j in range(NJ):
                E("gpsimd", (lambda qx=qx, j=j: lambda e: e.tensor_tensor(
                    junk4[:, j, :], xbuf[qx % 2][:, j, :],
                    xbuf[qx % 2][:, j, :], ALU.mult))(),
                  deps=[("xld", qx), ("ssq2", qx - 1, NJ - 1)],
                  inc=("gp", 1), label=("sqa2", qx, j))

        if "C" in stages and vq(qa):
            for m in range(KD):
                fns = [(lambda qa=qa, m=m, kk=kk: lambda e: e.matmul(
                    pA[m % 2][:], wP1["wga"][:, kk, m * P:(m + 1) * P],
                    ksl(vb[qa % 2], kk), start=(kk == 0), stop=(kk == KD - 1)))()
                    for kk in range(KD)]
                E("tensor", fns,
                  deps=[("thx", qa, 7), ("ga1", qa, m - 2), G1TOT, setup_dve],
                  inc=("pe", 1), label=("GA", qa, m))
                E("scalar", (lambda qa=qa, m=m: lambda e: e.activation(
                    ksl(abuf, m), pA[m % 2][:], AF.Tanh, bias=hbga_c(m),
                    scale=0.5))(),
                  deps=[("GA", qa, m), ("scan", qa - 1, 7),
                        ("sqa", qa - 1, m), setup_pe],
                  inc=("act", 1), label=("ga1", qa, m))
                E("scalar", (lambda qa=qa, m=m: lambda e: e.activation(
                    ksl(abuf, m), ksl(abuf, m), AF.Exp,
                    bias=whalf_c(m), scale=whalf_c(m)))(),
                  deps=["su4", setup_dve], inc=("act", 1), label=("aw", qa, m))
            E("vector", (lambda qa=qa: lambda e: e.scalar_tensor_tensor(
                thx[qa % 2][:], thx[qa % 2][:], 1.0, vb[qa % 2][:],
                ALU.add, ALU.mult))(),
              deps=[("GA", qa, 7), ("sphb", qa - 2)],
              inc=("dve", 1), label=("gxv", qa))

        if "E" in stages and vq(qg):
            for m in range(KD):
                fns = [(lambda qg=qg, m=m, kk=kk: lambda e: e.matmul(
                    pG[m % 2][:], wP1["wgx"][:, kk, m * P:(m + 1) * P],
                    ksl(vb[qg % 2], kk), start=(kk == 0), stop=(kk == KD - 1)))()
                    for kk in range(KD)]
                E("tensor", fns,
                  deps=[("vev", qg, 7), ("thx", qg, m - 2), G1TOT],
                  inc=("pe", 1), label=("GX", qg, m))
                E("scalar", (lambda qg=qg, m=m: lambda e: e.activation(
                    ksl(thx[qg % 2], m), pG[m % 2][:], AF.Tanh,
                    bias=hbgx_c(m), scale=0.5))(),
                  deps=[("GX", qg, m), ("sphb", qg - 2), setup_pe],
                  inc=("act", 1), label=("thx", qg, m))

        if "G" in stages and vq(qx):
            for j in range(NJ):
                E("vector", (lambda qx=qx, j=j: lambda e: e.tensor_reduce(
                    smv[qx % 2][:, j:j + 1], xbuf[qx % 2][:, j, :],
                    mybir.AxisListType.X, ALU.add))(),
                  deps=[("xld", qx)], inc=("dve", 1), label=("smv2", qx, j))
                E("vector", (lambda qx=qx, j=j: lambda e: e.tensor_reduce(
                    ssq[qx % 2][:, j:j + 1], junk4[:, j, :],
                    mybir.AxisListType.X, ALU.add))(),
                  deps=[("sqa2", qx, j)], inc=("dve", 1), label=("ssq2", qx, j))
            E("vector", (lambda qx=qx: lambda e: e.tensor_scalar(
                mvm[qx % 2][:], smv[qx % 2][:], 1.0 / 1024.0, None, ALU.mult))(),
              deps=[], inc=("dve", 1), label=("agM", qx))
            E("vector", (lambda qx=qx: lambda e: e.tensor_tensor(
                tmp4[qx % 2][:], mvm[qx % 2][:], mvm[qx % 2][:], ALU.mult))(),
              deps=[("nrm", qx - 2, NJ - 1)], inc=("dve", 1), label=("agE", qx))
            E("vector", (lambda qx=qx: lambda e: e.scalar_tensor_tensor(
                rstd[qx % 2][:], ssq[qx % 2][:], 1.0 / 1024.0, tmp4[qx % 2][:],
                ALU.mult, ALU.subtract))(),
              deps=[("ssq2", qx, NJ - 1), ("nrm", qx - 2, NJ - 1)],
              inc=("dve", 1), label=("agV", qx))
            E("vector", (lambda qx=qx: lambda e: e.tensor_scalar(
                rstd[qx % 2][:], rstd[qx % 2][:], EPS, None, ALU.add))(),
              deps=[], inc=("dve", 1), label=("veps", qx))
            E("scalar", (lambda qx=qx: lambda e: e.activation(
                tmp4[qx % 2][:], rstd[qx % 2][:], AF.Sqrt))(),
              deps=[("veps", qx)], inc=("act", 1), label=("rsq", qx))
            E("vector", (lambda qx=qx: lambda e: e.reciprocal(
                rstd[qx % 2][:], tmp4[qx % 2][:]))(),
              deps=[("rsq", qx)], inc=("dve", 1), label=("rcp", qx))
            E("vector", (lambda qx=qx: lambda e: e.scalar_tensor_tensor(
                tmp4[qx % 2][:], mvm[qx % 2][:], -1.0, rstd[qx % 2][:],
                ALU.mult, ALU.mult))(),
              deps=[("rcp", qx)], inc=("dve", 1), label=("ngm", qx))
            if dbg:
                E("scalar", (lambda qx=qx: lambda e: e.dma_start(
                    out=dbg_h[:, cdup(qx), 0:4], in_=mvm[qx % 2][:, 0:4]))(),
                  deps=[("agM", qx)], inc=("cp", 16))
                E("scalar", (lambda qx=qx: lambda e: e.dma_start(
                    out=dbg_h[:, cdup(qx), 4:8], in_=rstd[qx % 2][:, 0:4]))(),
                  deps=[("rcp", qx)], inc=("cp", 16))
                E("scalar", (lambda qx=qx: lambda e: e.dma_start(
                    out=dbg_h[:, cdup(qx), 8:12], in_=ssq[qx % 2][:, 0:4]))(),
                  deps=[("agV", qx)], inc=("cp", 16))

        if "I" in stages and vq(qx):
            for j in range(NJ):
                E("scalar", (lambda qx=qx, j=j: lambda e: e.activation(
                    xbuf[qx % 2][:, j, :], xbuf[qx % 2][:, j, :], AF.Identity,
                    bias=tmp4[qx % 2][:, j:j + 1],
                    scale=rstd[qx % 2][:, j:j + 1]))(),
                  deps=[("ngm", qx)], inc=("act", 1), label=("nrm", qx, j))

        if "F" in stages and vq(qv):
            for m in range(KD):
                fns = [(lambda qv=qv, m=m, kk=kk: lambda e: e.matmul(
                    pV[m % 2][:], wP1["wbx"][:, kk, m * P:(m + 1) * P],
                    ksl(xg[qv % 2], kk), start=(kk == 0), stop=(kk == KD - 1)))()
                    for kk in range(KD)]
                E("tensor", fns,
                  deps=[("xg", qv, 7), ("vev", qv, m - 2), G1TOT],
                  inc=("pe", 1), label=("V", qv, m))
                E("vector", (lambda qv=qv, m=m: lambda e: e.tensor_scalar(
                    ksl(vb[qv % 2], m), pV[m % 2][:], pk(BBX, m), None,
                    ALU.add))(),
                  deps=[("V", qv, m), ("GA", qv - 2, 7), ("gxv", qv - 2),
                        setup_gsp],
                  inc=("dve", 1), label=("vev", qv, m))
        if "H" in stages and vq(qa):
            for k in range(KD):
                E("scalar", (lambda qa=qa, k=k: lambda e: e.activation(
                    sq[k % 2][:], ksl(abuf, k), AF.Square))(),
                  deps=[("aw", qa, k), ("si", qa, k - 2), setup_pe],
                  inc=("act", 1), label=("sqa", qa, k))
                E("scalar", (lambda qa=qa, k=k: lambda e: e.activation(
                    sq[k % 2][:], sq[k % 2][:], AF.Sqrt,
                    bias=q25[:, 0:1], scale=-0.25))(),
                  deps=[("sqa", qa, k), "q25", ("si", qa, k - 2)],
                  inc=("act", 1), label=("sc2", qa, k))
                E("vector", (lambda qa=qa, k=k: lambda e: e.tensor_tensor(
                    ksl(thx[qa % 2], k), ksl(thx[qa % 2], k), sq[k % 2][:],
                    ALU.mult))(),
                  deps=[("sc2", qa, k), ("gxv", qa)],
                  inc=("dve", 1), label=("si", qa, k))
            for k in range(KD):
                ini = 0.0 if qa == 0 else carry[:, k * 32:k * 32 + 1]
                E("vector", (lambda qa=qa, k=k, ini=ini: lambda e:
                             e.tensor_tensor_scan(
                                 ksl(thx[qa % 2], k)[:, ::-1],
                                 ksl(abuf, k)[:, ::-1],
                                 ksl(thx[qa % 2], k)[:, ::-1],
                                 ini, ALU.mult, ALU.add))(),
                  deps=[("si", qa, k), ("cv", qa - 1)],
                  inc=("dve", 1), label=("scan", qa, k))
            E("vector", (lambda qa=qa: lambda e: e.tensor_copy(
                carry[:].rearrange("p (k n) -> p k n", k=KD),
                thx[qa % 2][:].rearrange("p (k n) -> p k n", k=KD)
                [:, :, 0:32]))(),
              deps=[("scan", qa, 7)], inc=("dve", 1), label=("cv", qa))

        if "W" in stages and i == NCH - 1:
            for nm, g in (("wf1", [("T", NCH - 1, 7)]),
                          ("wf2", [("V", NCH - 1, 7), ("spxg", NCH - 1)]),
                          ("wb2", [("V", NCH - 1, 7)])):
                for k in range(KD):
                    E("gpsimd", (lambda nm=nm, k=k: lambda e: e.dma_start(
                        out=wP2[nm][:, k, :], in_=w_r[nm][k]))(),
                      deps=g, inc=("g2", 16), label=("w2", nm, k))

    P1_LAST = [("sphb", NCH - 1)]

    for k in range(KD):
        E("sync", (lambda k=k: lambda e: e.dma_start(
            out=wP2["wc"][:, k, :], in_=wc_h[:, k, :]))(),
          deps=[("T", NCH - 1, 7), ("#", "gsp", 32)],
          inc=("g3", 16), label=("w2", "wc", k))
    G2TOT = ("#", "g2", pl.cnt["g2"])
    G3TOT = ("#", "g3", pl.cnt["g3"])
    E("sync", lambda e: e.dma_start(out=afw[:], in_=afwbc_h),
      deps=[("si", NCH - 1, 7), ("sc2", NCH - 1, 7), ("th", NCH - 1, 7),
            ("xg", NCH - 1, 7), ("ssq2", NCH - 1, NJ - 1)],
      inc=("x2", 16), label="afb")
    JG = ("ssq2", NCH - 1, NJ - 1)

    for i in range(-1, (NCH + 1) if ph2 else -1):
        jl = i + 1
        jc = i
        jf = i - 1

        def ofob_half(h, jf):
            for m in range(h * 4, h * 4 + 4):
                fns = [(lambda jf=jf, m=m, kk=kk: lambda e: e.matmul(
                    pS[m % 2][:], wP2["wf2"][:, kk, m * P:(m + 1) * P],
                    ksl(hfb[jf % 2], kk), start=(kk == 0), stop=False))()
                    for kk in range(KD)]
                fns += [(lambda jf=jf, m=m, kk=kk: lambda e: e.matmul(
                    pS[m % 2][:], wP2["wb2"][:, kk, m * P:(m + 1) * P],
                    ksl(hbin[jf % 2], kk), start=False, stop=(kk == KD - 1)))()
                    for kk in range(KD)]
                E("tensor", fns,
                  deps=[("hf", jf), ("p2hb", jf), ("ssum", jf, m - 2), G2TOT],
                  inc=("pe", 1), label=("OF", jf, m))
                E("vector", (lambda jf=jf, m=m: lambda e:
                             e.scalar_tensor_tensor(
                                 ssx[m % 2][:], pS[m % 2][:],
                                 dsum[:, m:m + 1], ksl(xnbin[jf % 2], m),
                                 ALU.add, ALU.add))(),
                  deps=[("OF", jf, m), ("prod", jf, m - 2),
                        ("prod", jf - 1, m), "dsu"]
                       + ([JG] if jf == 0 else []),
                  inc=("dve", 1), label=("ssum", jf, m))
                E("vector", (lambda jf=jf, m=m: lambda e: e.tensor_tensor(
                    prs[m % 4][:], ssx[m % 2][:], ksl(gbuf, m), ALU.mult))(),
                  deps=[("ssum", jf, m), ("gev", jf, m)]
                       + ([("Tp", jf, 3)] if m >= 4
                          else [("Tp", jf - 1, 7)]),
                  inc=("dve", 1), label=("prod", jf, m))
        if "o" in stages and vq(jf):
            ofob_half(0, jf)

        if "u" in stages and vq(jc):
            for m in range(KD):
                fns = [(lambda jc=jc, m=m, kk=kk: lambda e: e.matmul(
                    pU[m % 2][:], wP2["wf1"][:, kk, m * P:(m + 1) * P],
                    ksl(xgin[jc % 2], kk), start=(kk == 0), stop=(kk == KD - 1)))()
                    for kk in range(KD)]
                E("tensor", fns,
                  deps=[("p2xg", jc), ("uev", jc, m - 2), G2TOT]
                       + ([("thx", NCH - 1, 7)] if jc == 0 else []),
                  inc=("pe", 1), label=("U", jc, m))
                E("scalar", (lambda jc=jc, m=m: lambda e: e.activation(
                    ksl(ubuf, m), pU[m % 2][:], AF.Identity,
                    bias=pk(BF1, m)))(),
                  deps=[("U", jc, m), ("p2cv", jc - 1), ("hf", jc - 1)]
                       + ([("gxv", NCH - 1), ("GA", NCH - 1, 7)]
                          if jc == 0 else []),
                  inc=("act", 1), label=("uev", jc, m))
            for k in range(KD):
                ini = 0.0 if jc == 0 else carry[:, k * 32 + 31:k * 32 + 32]
                E("vector", (lambda k=k, ini=ini, jc=jc: lambda e:
                             e.tensor_tensor_scan(
                                 ksl(ubuf, k), ksl(afw, k), ksl(ubuf, k),
                                 ini, ALU.mult, ALU.add))(),
                  deps=[("uev", jc, 7), "afb", ("p2cv", jc - 1)],
                  inc=("dve", 1), label=("p2scan", jc, k))
            E("vector", (lambda jc=jc: lambda e: e.tensor_copy(
                carry[:].rearrange("p (k n) -> p k n", k=KD),
                ubuf[:].rearrange("p (k n) -> p k n", k=KD)
                [:, :, SC - 32:SC]))(),
              deps=[("p2scan", jc, 7)], inc=("dve", 1), label=("p2cv", jc))
            E("scalar", (lambda jc=jc: lambda e: e.activation(
                hfb[jc % 2][:], ubuf[:], AF.Identity))(),
              deps=[("p2scan", jc, 7), ("OF", jc - 2, 7)],
              inc=("act", 1), label=("hf", jc))

        def tp_half(h, jf):
            for jj in range(NJ):
                g = h * 4 + jj
                bank = [pT0, pP2][g % 2]
                fns = [(lambda jj=jj, kk=kk, bank=bank: lambda e: e.transpose(
                    bank[:, kk * P:(kk + 1) * P],
                    prs[kk][:, jj * P:(jj + 1) * P], ident[:]))()
                    for kk in range(4)]
                E("tensor", fns,
                  deps=[("prod", jf, h * 4 + 3), ("oadd", jf, g - 2), "Lcp"],
                  inc=("pe", 1), label=("Tp", jf, g))
                E("vector", (lambda jf=jf, jj=jj, h=h, bank=bank: lambda e:
                             e.tensor_tensor(
                                 xin[jf % 2][:, jj, h * SC:(h + 1) * SC],
                                 xin[jf % 2][:, jj, h * SC:(h + 1) * SC],
                                 bank[:], ALU.add))(),
                  deps=[("Tp", jf, h * 4 + jj), ("p2x", jf)],
                  inc=("dve", 1), label=("oadd", jf, h * 4 + jj))
        if "t" in stages and vq(jf):
            tp_half(0, jf)
        if "o" in stages and vq(jf):
            ofob_half(1, jf)

        if "f" in stages and vq(jc):
            for m in range(KD):
                fns = [(lambda jc=jc, m=m, kk=kk: lambda e: e.matmul(
                    pF[m % 2][:], wP2["wc"][:, kk, m * P:(m + 1) * P],
                    ksl(xnbin[jc % 2], kk), start=(kk == 0), stop=(kk == KD - 1)))()
                    for kk in range(KD)]
                E("tensor", fns,
                  deps=[("p2xnb", jc), ("gev", jc, m - 2), G3TOT],
                  inc=("pe", 1), label=("FF", jc, m))
                E("scalar", (lambda jc=jc, m=m: lambda e: e.activation(
                    ksl(gbuf, m), pF[m % 2][:], AF.Silu, bias=pk(BC, m)))(),
                  deps=[("FF", jc, m), ("prod", jc - 1, m)]
                       + ([JG] if jc == 0 else []),
                  inc=("act", 1), label=("gev", jc, m))

        if "t" in stages and vq(jf):
            tp_half(1, jf)

        if "y" in stages and vq(jf):
            E("scalar", (lambda jf=jf: lambda e: e.dma_start(
                out=y_r[jf], in_=xin[jf % 2][:]))(),
              deps=[("oadd", jf, 7), ("yst", jf - 1)],
              inc=("yst", 16), label=("yst", jf))

        if "s" in stages and vq(jl):
            E("sync", (lambda jl=jl: lambda e: e.dma_start(
                out=xgin[jl % 2][:], in_=xg_h[:, jl, :]))(),
              deps=[("U", jl - 2, 7), ("OF", jl - 2, 7), ("p2xg", jl - 1),
                    ("GX", NCH - 1, 7), ("spxg", NCH - 1 - jl)],
              inc=("p2g", 16), label=("p2xg", jl))
            E("sync", (lambda jl=jl: lambda e: e.dma_start(
                out=xnbin[jl % 2][:], in_=xnb_h[:, jl, :]))(),
              deps=[("FF", jl - 2, 7), ("ssum", jl - 2, 7), ("p2xnb", jl - 1),
                    ("GA", NCH - 1, 7), ("spxnb", NCH - 1 - jl)],
              inc=("p2n", 16), label=("p2xnb", jl))
            E("sync", (lambda jl=jl: lambda e: e.dma_start(
                out=hbin[jl % 2][:], in_=hb_h[:, jl, :]))(),
              deps=[("OF", jl - 2, 7), ("p2hb", jl - 1),
                    ("sphb", NCH - 1 - jl)],
              inc=("p2h", 16), label=("p2hb", jl))
            E("sync", (lambda jl=jl: lambda e: e.dma_start(
                out=xin[jl % 2][:], in_=x_r[jl]))(),
              deps=[("yst", jl - 2), ("p2x", jl - 1),
                    ("scan", NCH - 1, 7), ("sqa", NCH - 1, 7)] + P1_LAST,
              inc=("x2", 16), label=("p2x", jl))

    fwd = [d for d in pl.skipped if d in pl.ev]
    assert not fwd, f"forward-referenced deps (ordering bugs): {sorted(set(fwd))[:20]}"
    build_nc._last_plan = pl

    E("sync", lambda e: e.nop(),
      deps=[("#", "yst", pl.cnt["yst"]), ("#", "shb", pl.cnt["shb"]),
            ("#", "sxg", pl.cnt["sxg"]), ("#", "sxnb", pl.cnt["sxnb"]),
            ("#", "gsp", pl.cnt["gsp"]), ("#", "g2", pl.cnt["g2"]),
            ("#", "g3", pl.cnt["g3"])], inc=None)

    es = contextlib.ExitStack()
    with es:
        sems = {s: es.enter_context(nc.semaphore(f"sem_{s}")) for s in SEMS}
        block = es.enter_context(nc.Block())

        @block.sync
        def _(eng):
            pl.run("sync", eng, sems)

        @block.scalar
        def _(eng):
            pl.run("scalar", eng, sems)

        @block.vector
        def _(eng):
            pl.run("vector", eng, sems)

        @block.tensor
        def _(eng):
            pl.run("tensor", eng, sems)

        @block.gpsimd
        def _(eng):
            pl.run("gpsimd", eng, sems)

    return nc


def kernel(**inputs):
    x = np.asarray(inputs["L0_attn_x"], dtype=np.float32)
    vecs = [np.asarray(inputs[k], dtype=np.float32) for k in (
        "L0_attn_bf1", "L0_attn_bf2", "L0_attn_bbx", "L0_attn_bgx",
        "L0_attn_bga", "L0_attn_bb2", "L0_ffn_b", "L0_attn_a_fwd",
        "L0_attn_a_param")]
    packs = np.concatenate([v.reshape(KD, P).T for v in vecs], axis=1)
    packs = np.ascontiguousarray(packs, dtype=np.float32)
    base = {
        "wf1": np.ascontiguousarray(inputs["L0_attn_Wf1"], np.float32),
        "wf2": np.ascontiguousarray(inputs["L0_attn_Wf2"], np.float32),
        "wbx": np.ascontiguousarray(inputs["L0_attn_Wbx"], np.float32),
        "wgx": np.ascontiguousarray(inputs["L0_attn_Wgx"], np.float32),
        "wga": np.ascontiguousarray(inputs["L0_attn_Wga"], np.float32),
        "wb2": np.ascontiguousarray(inputs["L0_attn_Wb2"], np.float32),
        "wT": np.ascontiguousarray(np.asarray(inputs["L0_ffn_W"]).T, np.float32),
        "dna": np.ascontiguousarray(inputs["L0_ffn_down_aux"], np.float32),
        "uaT": np.ascontiguousarray(np.asarray(inputs["L0_ffn_up_aux"]).T,
                                    np.float32),
        "ldT": np.ascontiguousarray(np.asarray(inputs["L0_ffn_lora_down"]).T,
                                    np.float32),
        "luT": np.ascontiguousarray(np.asarray(inputs["L0_ffn_lora_up"]).T,
                                    np.float32),
        "packs": packs,
        "ident": np.eye(P, dtype=np.float32),
        "afwbc": np.ascontiguousarray(np.repeat(
            np.asarray(inputs["L0_attn_a_fwd"], np.float32).reshape(KD, P).T
            [:, :, None], SC, axis=2).reshape(P, KD * SC)),
    }
    nc = build_nc()
    in_maps = []
    for b in range(8):
        m = dict(base)
        m["x"] = np.ascontiguousarray(x[b], np.float32)
        in_maps.append(m)
    res = run_bass_kernel_spmd(nc, in_maps, core_ids=list(range(8)))
    return np.stack([r["y"] for r in res.results], axis=0)
